# version 05
# baseline (speedup 1.0000x reference)
import math
import numpy as np
import ml_dtypes
import concourse.bass as bass
import concourse.mybir as mybir
from concourse.bass_utils import run_bass_kernel_spmd

F32 = mybir.dt.float32
BF16 = mybir.dt.bfloat16
Alu = mybir.AluOpType
Act = mybir.ActivationFunctionType
AX = mybir.AxisListType.X

D = 2048
T = 4096
B = 4
H = 16
DH = 128
II = 8192
NCK = 16
NQB = 16
TOWN = NQB * 128
EPS = 1e-5
ISQ = 1.0 / math.sqrt(128.0)

SLOPES = [2.0 ** (-(h + 1) * 0.5) for h in range(H)]
WTILE = [min(512, max(128, int(170.0 / s) // 128 * 128)) for s in SLOPES]
WIN = [min(T, (int(80.0 / s) // 128 + 1) * 128) for s in SLOPES]

PHASE_SET = {"mod", "stats", "qkv", "attn", "oproj", "h2", "wq", "gate", "up",
             "down"}
DEBUG_OUTS = set()


def attn_tiles(n, h):
    w = WTILE[h]
    kend = (2 * n + 2) * 128
    g128 = (2 * n + 1) * 128
    tiles = []
    m = 1
    while True:
        j0 = kend - m * w
        if j0 + w <= 0:
            break
        if g128 - (j0 + w - 1) > WIN[h]:
            break
        tiles.append((j0, w, m))
        m += 1
    return list(reversed(tiles))


def _bias_cols():
    keys = []
    for h in range(H):
        maxm = 0
        for n in range(NQB):
            for (_, _, m) in attn_tiles(n, h):
                maxm = max(maxm, m)
        for m in range(1, maxm + 1):
            keys.append((h, m))
    return {k: i for i, k in enumerate(keys)}


BIAS_COLS = _bias_cols()


def _f_cols():
    keys = []
    for h in range(H):
        for c in range(WTILE[h] // 128):
            keys.append((h, c))
    return {k: i for i, k in enumerate(keys)}


F_COLS = _f_cols()


def host_tables(p):
    prow = np.arange(128, dtype=np.float64)
    bt = np.zeros((128, len(BIAS_COLS)), dtype=np.float32)
    for (h, m), col in BIAS_COLS.items():
        s, w = SLOPES[h], WTILE[h]
        bt[:, col] = (-s * (prow + m * w + (p - 2) * 128 - w / 2.0)).astype(np.float32)
        if p == 0 and w == 128 and m == 1:
            bt[:, col] = -30000.0
    ft = np.zeros((128, len(F_COLS)), dtype=np.float32)
    for (h, c), col in F_COLS.items():
        s, w = SLOPES[h], WTILE[h]
        ft[:, col] = np.exp(s * (prow + c * 128 - w / 2.0)).astype(np.float32)
    tri = np.tril(np.ones((128, 128), dtype=np.float32))
    msk = np.zeros((128, 256), dtype=np.float32)
    if p == 1:
        msk[:, 0:128] = 1.0
        msk[:, 128:256] = tri
    else:
        msk[:, 0:128] = tri
    return bt, ft, msk


def own_rows(p):
    idx = []
    for n in range(NQB):
        g = 2 * n + p
        idx.extend(range(g * 128, (g + 1) * 128))
    return np.array(idx)



class Plan:
    ENGS = ("sync", "gpsimd", "tensor", "vector", "scalar")

    DMA_SLOTS = 4
    SEMS = (["sS%d" % i for i in range(4)] + ["sG%d" % i for i in range(4)]
            + ["sT", "sV", "sC", "sP"])

    def __init__(self):
        self.ops = {k: [] for k in self.ENGS}
        self.cnt = {k: 0 for k in self.SEMS}
        self.prog_sem = {"tensor": "sT", "vector": "sV", "scalar": "sC",
                         "gpsimd": "sP"}

    def dma(self, eng, out, in_, w=(), slot=0, **kw):
        sem = ("sS%d" if eng == "sync" else "sG%d") % (slot % self.DMA_SLOTS)
        ww = tuple(w) + ((sem, self.cnt[sem]),)
        self.cnt[sem] += 16
        self.ops[eng].append(("dma_start", (), dict(out=out, in_=in_, **kw),
                              ww, ((sem, 16),)))
        return (sem, self.cnt[sem])

    def op(self, eng, name, *a, w=(), **kw):
        sem = self.prog_sem[eng]
        self.cnt[sem] += 1
        self.ops[eng].append((name, a, kw, tuple(w), ((sem, 1),)))
        return (sem, self.cnt[sem])

    def wait(self, eng, *toks):
        for t in toks:
            if t is None:
                continue
            (s, v) = t
            if v > 0:
                self.ops[eng].append(("wait_ge", (s, v), {}, (), ()))

    def emit(self, eng_obj, eng_name, sems):
        for (name, a, kw, waits, incs) in self.ops[eng_name]:
            if name == "wait_ge":
                eng_obj.wait_ge(sems[a[0]], a[1])
                continue
            best = {}
            for (s, v) in waits:
                if v > 0 and (s not in best or best[s] < v):
                    best[s] = v
            items = sorted(best.items())
            for s, v in items[1:]:
                eng_obj.wait_ge(sems[s], v)
            inst = getattr(eng_obj, name)(*a, **kw)
            if items:
                inst._wait_ge(sems[items[0][0]], items[0][1])
            for (s, v) in incs:
                inst.then_inc(sems[s], v)


def tmax(*toks):
    best = {}
    for t in toks:
        if t is None:
            continue
        if isinstance(t, list):
            for tt in t:
                s, v = tt
                if s not in best or best[s] < v:
                    best[s] = v
            continue
        s, v = t
        if s not in best or best[s] < v:
            best[s] = v
    return [(s, v) for s, v in best.items()]



def build():
    nc = bass.Bass("TRN2", target_bir_lowering=False)
    dram = {}

    def din(name, shape, dt=F32):
        dram[name] = nc.dram_tensor(name, shape, dt, kind="ExternalInput").ap()
        return dram[name]

    def dout(name, shape, dt=F32):
        dram[name] = nc.dram_tensor(name, shape, dt, kind="ExternalOutput").ap()
        return dram[name]

    def dscr(name, shape, dt=F32):
        kind = "ExternalOutput" if name in DEBUG_OUTS else "Internal"
        dram[name] = nc.dram_tensor(name, shape, dt, kind=kind).ap()
        return dram[name]

    xfm = din("xfm", [D, T])
    xfo = din("xfo", [D, TOWN])
    xtm = din("xtm", [TOWN, D])
    wqT = din("wqT", [D, D])
    wkT = din("wkT", [D, D])
    wvT = din("wvT", [D, D])
    woT = din("woT", [D, D])
    wg = din("wg", [II, D])
    wu = din("wu", [II, D])
    wd = din("wd", [D, II])
    wmod = din("wmod", [D, 6 * D])
    ccol = din("ccol", [128, NCK])
    btab = din("btab", [128, len(BIAS_COLS)])
    ftab = din("ftab", [128, len(F_COLS)])
    mtab = din("mtab", [128, 256])
    identb = din("identb", [128, 128], BF16)
    onesrow = din("onesrow", [1, T], BF16)
    identf = din("identf", [128, 128], F32)

    y = dout("y", [D, TOWN])
    mod_s = dscr("mod_s", [1, 6 * D])
    silc_s = dscr("silc_s", [128, NCK])
    xhat_s = dscr("xhat_s", [D, T], BF16)
    xhato_s = dscr("xhato_s", [D, TOWN], BF16)
    r1o_s = dscr("r1o_s", [1, TOWN])
    m1no_s = dscr("m1no_s", [1, TOWN])
    r1_s = dscr("r1_s", [1, T])
    m1n_s = dscr("m1n_s", [1, T])
    garow_s = dscr("garow_s", [1, D])
    gbrow_s = dscr("gbrow_s", [1, D])
    cbrow_s = dscr("cbrow_s", [1, D])
    sbrow_s = dscr("sbrow_s", [1, D])
    q_scr = dscr("q_scr", [D, TOWN], BF16)
    k_scr = dscr("k_scr", [D, T], BF16)
    v_scr = dscr("v_scr", [D, T], BF16)
    x2_s = dscr("x2_s", [TOWN, D])
    x2fm_s = dscr("x2fm_s", [D, TOWN])
    sg_s = dscr("sg_s", [TOWN, II], BF16)
    qh2_s = dscr("qh2_s", [D, TOWN], BF16)
    qint_s = dscr("qint_s", [II, TOWN], BF16)
    qwg_s = dscr("qwg_s", [D, II], BF16)
    qwu_s = dscr("qwu_s", [D, II], BF16)
    qwd_s = dscr("qwd_s", [II, D], BF16)

    from contextlib import ExitStack
    stack = ExitStack()
    e = stack.enter_context
    sb = {}

    def S(name, shape, dt=F32):
        sb[name] = e(nc.sbuf_tensor(name, shape, dt))
        return sb[name]

    def P(name, shape, dt=F32):
        sb[name] = e(nc.psum_tensor(name, shape, dt))
        return sb[name]

    wres = [S(f"wres{i}", [128, D], BF16) for i in range(NCK)]
    F32R = mybir.dt.float32r
    wld = [S(f"wld{i}", [128, 4096], F32R) for i in range(2)]
    xb = [S(f"xb{i}", [128, NCK * 512], BF16) for i in range(2)]
    xg = [S(f"xg{i}", [128, NCK * 512], BF16) for i in range(2)]
    m1o = S("m1o", [2, T], BF16)
    m1oo = S("m1oo", [2, TOWN], BF16)
    wbar2 = S("wbar2", [2, D], BF16)
    linv = S("linv", [128, 1])
    qblk = S("qblk", [128, 448])
    stt2 = S("stt2", [128, 8])
    qbrow = S("qbrow", [1, D], BF16)
    sapb = S("sapb", [128, NCK], BF16)
    rrep = [S(f"rrep{i}", [128, 512]) for i in range(2)]
    ost = [S(f"ost{i}", [128, 512], BF16) for i in range(4)]
    sq = [ost[0], ost[1]]
    rowa = S("rowa", [1, 512])
    rowb = S("rowb", [1, 512])
    rowm = S("rowm", [1, 512])
    rowr = S("rowr", [1, 512])
    rowmn = S("rowmn", [1, 512])
    rown = rowa
    wbar = S("wbar", [1, D], BF16)
    qbc = S("qbc", [128, NCK])
    modc = S("modc", [128, NCK * 6])
    opca = S("opca", [128, NCK])
    sap = S("sap", [128, NCK])
    gacl = S("gacl", [128, NCK])
    gbcl = S("gbcl", [128, NCK])
    opcb = S("opcb", [128, NCK])
    sbc = S("sbc", [128, NCK])
    silc = S("silc", [128, NCK])
    silcb = S("silcb", [128, NCK], BF16)
    silcr = S("silcr", [128, NCK], mybir.dt.float32r)
    tmpc = S("tmpc", [128, NCK])
    tmpc2 = S("tmpc2", [128, NCK])
    idb = S("idb", [128, 128], BF16)
    idf = S("idf", [128, 128])
    bt_sb = S("bt_sb", [128, len(BIAS_COLS)])
    ft_sb = S("ft_sb", [128, len(F_COLS)])
    mk_sb = S("mk_sb", [128, 256], BF16)
    onesb = S("onesb", [128, 1], BF16)

    psA = [P(f"psA{i}", [128, 512]) for i in range(4)]
    psB = [P(f"psB{i}", [128, 512], BF16) for i in range(2)]
    psR = P("psR", [1, 512])
    psR2 = P("psR2", [1, 512])

    pl = Plan()

    def DS(out, in_, w=(), slot=0, **kw):
        return pl.dma("sync", out, in_, w=w, slot=slot, **kw)

    def DG(out, in_, w=(), slot=0, **kw):
        return pl.dma("gpsimd", out, in_, w=w, slot=slot, **kw)

    def V(name, *a, w=(), **kw):
        return pl.op("vector", name, *a, w=w, **kw)

    def Cs(name, *a, w=(), **kw):
        return pl.op("scalar", name, *a, w=w, **kw)

    def TE(name, *a, w=(), **kw):
        return pl.op("tensor", name, *a, w=w, **kw)

    def G(name, *a, w=(), **kw):
        return pl.op("gpsimd", name, *a, w=w, **kw)

    ph = PHASE_SET

    tb1 = DS(out=idb[:], in_=identb)
    tb2 = DS(out=bt_sb[:], in_=btab)
    tb3 = DS(out=ft_sb[:], in_=ftab)
    tb4 = DG(out=mk_sb[:], in_=mtab)
    tb5 = DS(out=tmpc[:], in_=ccol)
    tb6 = DS(out=idf[:], in_=identf)
    mz = V("memset", onesb[:], 1.0)

    mod_done = None
    mod_xb_guard = None
    mod_psr_guard = None
    if "mod" in ph:
        silu_tok = Cs("activation", silc[:], tmpc[:], Act.Silu, w=[tb5])
        sil_st = DS(out=silc_s, in_=silc[:], w=[silu_tok], slot=1)
        cast_tok = DS(out=silcr[:], in_=silc_s.bitcast(mybir.dt.float32r),
                      w=[sil_st], slot=1)
        n_jt = 6 * D // 512
        st_prev = None
        ev_prev = None
        mm_tok = None
        last_buf_use = {0: None, 1: None}
        st_toks = []
        wmod_r = wmod.bitcast(mybir.dt.float32r).rearrange("(c p) j -> p c j", p=128)
        for jt in range(n_jt):
            for hf in range(2):
                bi = (jt * 2 + hf) % 2
                xv = wld[bi].ap().rearrange("p (c t) -> p c t", t=512)
                ld = DS(out=xv,
                        in_=wmod_r[:, hf * 8:(hf + 1) * 8,
                                   jt * 512:(jt + 1) * 512],
                        w=tmax(last_buf_use[bi], cast_tok), slot=bi)
                for dci in range(8):
                    dc = hf * 8 + dci
                    tile = wld[bi].ap()[:, dci * 512:(dci + 1) * 512]
                    mm_tok = TE("matmul", psR[:], silcr[:, dc:dc + 1], tile,
                                start=(dc == 0), stop=(dc == NCK - 1),
                                w=tmax(ld, mm_tok,
                                       ev_prev if dc == 0 else None))
                last_buf_use[bi] = mm_tok
            ev_prev = V("tensor_copy", rown[:], psR[:], w=tmax(mm_tok, st_prev))
            st_prev = DS(out=mod_s[0:1, jt * 512:(jt + 1) * 512], in_=rown[:],
                         w=[ev_prev], slot=1)
            st_toks.append(st_prev)
        mod_xb_guard = mm_tok
        mod_psr_guard = ev_prev
        mod_ld = DS(out=modc[:],
                    in_=mod_s.rearrange("one (c p) -> (one p) c", p=128),
                    w=tmax(*st_toks), allow_slow_non_contiguous=True, slot=1)
        sa_c = modc.ap()[:, 0:16]
        ca_c = modc.ap()[:, 16:32]
        ga_c = modc.ap()[:, 32:48]
        sb_c2 = modc.ap()[:, 48:64]
        cb_c = modc.ap()[:, 64:80]
        gb_c = modc.ap()[:, 80:96]
        t1 = V("tensor_scalar", opca[:], ca_c, 1.0, None, Alu.add, w=[mod_ld])
        t2 = V("reciprocal", tmpc2[:], opca[:], w=[t1])
        t3 = V("tensor_tensor", sap[:], sa_c, tmpc2[:], Alu.mult, w=[t2])

        def clamp(dst, src, w):
            a0 = V("tensor_scalar", tmpc2[:], src, -1.0, None, Alu.mult, w=w)
            a = V("tensor_tensor", tmpc2[:], tmpc2[:], src, Alu.max, w=[a0])
            b = V("tensor_scalar", tmpc2[:], tmpc2[:], 0.25, None, Alu.max, w=[a])
            c0 = V("tensor_scalar", dst, src, 0.0, 2.0, Alu.is_ge, Alu.mult,
                   w=w)
            c1 = V("tensor_scalar", dst, dst, -1.0, None, Alu.add, w=[c0])
            return V("tensor_tensor", dst, dst, tmpc2[:], Alu.mult, w=[b, c1])

        g1 = clamp(gacl[:], ga_c, [t3])
        g2 = clamp(gbcl[:], gb_c, [g1])
        t4 = V("tensor_scalar", opcb[:], cb_c, 1.0, None, Alu.add, w=[g2])
        t5 = V("tensor_copy", sbc[:], sb_c2, w=[t4])
        r1 = DS(out=garow_s.rearrange("one (c p) -> (one p) c", p=128),
                in_=gacl[:], w=[g1], allow_slow_non_contiguous=True, slot=2)
        r2 = DS(out=gbrow_s.rearrange("one (c p) -> (one p) c", p=128),
                in_=gbcl[:], w=[g2], allow_slow_non_contiguous=True, slot=2)
        r3 = DS(out=cbrow_s.rearrange("one (c p) -> (one p) c", p=128),
                in_=opcb[:], w=[t4], allow_slow_non_contiguous=True, slot=2)
        r4 = DS(out=sbrow_s.rearrange("one (c p) -> (one p) c", p=128),
                in_=sbc[:], w=[t5], allow_slow_non_contiguous=True, slot=2)
        mod_done = tmax(t3, t5, r1, r2, r3, r4)

    stats_done = None
    stats_xb_guard = None

    def emit_stats(src, ntt, r_dst, mn_dst, xh_dst, st0):
        st = dict(sq_t={0: st0.get("sq0"), 1: st0.get("sq1")},
                  ev1=None, ev2=None, st1=None,
                  st2=None, xh_st=None, rr_use={0: None, 1: None},
                  buf_use={0: st0["xb0"], 1: st0["xb1"]},
                  psr_g=st0["psr"], psr2_g=st0["psr2"], row_g=st0["row"],
                  xh_g=st0["xh"])
        src_r = src.rearrange("(c p) t -> p c t", p=128)
        xhd_r = xh_dst.rearrange("(c p) t -> p c t", p=128)
        for tt in range(ntt):
            bi = tt % 2
            xt = xb[bi]
            ld0 = DG(out=xt.ap().rearrange("p (c t) -> p c t", t=512),
                     in_=src_r[:, :, tt * 512:(tt + 1) * 512],
                     w=tmax(st["buf_use"][bi]), slot=bi)
            smm = None
            qmm = None
            sqt = None
            for dc in range(NCK):
                xtile = xt.ap()[:, dc * 512:(dc + 1) * 512]
                sbi = dc % 2
                smm = TE("matmul", psR[:], onesb[:], xtile,
                         start=(dc == 0), stop=(dc == NCK - 1),
                         w=tmax(ld0, mz, smm,
                                st["ev1"] if dc == 0 else None,
                                st["psr_g"] if dc == 0 else None))
                sqt = Cs("activation", sq[sbi][:], xtile, Act.Square,
                         w=tmax(ld0, st["sq_t"][sbi]))
                qmm = TE("matmul", psR2[:], onesb[:], sq[sbi][:],
                         start=(dc == 0), stop=(dc == NCK - 1),
                         w=tmax(sqt, qmm, st["ev2"] if dc == 0 else None,
                                st["psr2_g"] if dc == 0 else None))
                st["sq_t"][sbi] = qmm
            st["psr_g"] = None
            st["psr2_g"] = None
            ev1 = V("tensor_copy", rowa[:], psR[:],
                    w=tmax(smm, st["st1"], st["row_g"], st0.get("rowa_g")))
            ev2 = V("tensor_copy", rowb[:], psR2[:], w=tmax(qmm, st["st2"]))
            st["row_g"] = None
            f1 = V("tensor_scalar", rowm[:], rowa[:], 1.0 / D, None, Alu.mult,
                   w=[ev1])
            f2 = V("tensor_scalar", rowb[:], rowb[:], 1.0 / D, None, Alu.mult,
                   w=[ev2])
            f3 = Cs("activation", rowa[:], rowm[:], Act.Square, w=[f1])
            f4 = V("tensor_tensor", rowb[:], rowb[:], rowa[:], Alu.subtract,
                   w=[f3, f2])
            f5 = V("tensor_scalar", rowb[:], rowb[:], EPS, None, Alu.add,
                   w=[f4])
            f6 = V("reciprocal", rowa[:], rowb[:], w=[f5])
            f7 = Cs("activation", rowr[:], rowa[:], Act.Sqrt, w=[f6])
            f8 = V("tensor_tensor", rowmn[:], rowm[:], rowr[:], Alu.mult,
                   w=[f7])
            f9 = V("tensor_scalar", rowmn[:], rowmn[:], -1.0, None, Alu.mult,
                   w=[f8])
            st["st1"] = DS(out=r_dst[0:1, tt * 512:(tt + 1) * 512],
                           in_=rowr[:], w=[f7], slot=3)
            st["st2"] = DS(out=mn_dst[0:1, tt * 512:(tt + 1) * 512],
                           in_=rowmn[:], w=[f9], slot=3)
            rb = DS(out=rrep[bi][:],
                    in_=r_dst[0:1, tt * 512:(tt + 1) * 512]
                    .broadcast_to([128, 512]),
                    w=tmax(st["st1"], st["rr_use"][bi]), slot=2)
            xh_toks = []
            rd_done = tmax(smm, qmm, sqt, st["sq_t"][0], st["sq_t"][1])
            for dc in range(NCK):
                xtile = xt.ap()[:, dc * 512:(dc + 1) * 512]
                xh_toks.append(
                    V("tensor_tensor", xtile, xtile, rrep[bi][:], Alu.mult,
                      w=tmax(rb, rd_done)))
            st["xh_st"] = DG(out=xhd_r[:, :, tt * 512:(tt + 1) * 512],
                             in_=xt.ap().rearrange("p (c t) -> p c t", t=512),
                             w=tmax(*xh_toks), slot=2 + (tt % 2))
            st["rr_use"][bi] = tmax(*xh_toks)
            st["buf_use"][bi] = tmax(*xh_toks, st["xh_st"])
        return st

    if "stats" in ph:
        st0 = dict(xb0=mod_xb_guard, xb1=mod_xb_guard, psr=mod_psr_guard,
                   psr2=None, row=None, xh=None, rowa_g=mod_done)
        stA = emit_stats(xfm, T // 512, r1_s, m1n_s, xhat_s, st0)
        st1 = dict(xb0=stA["buf_use"][0], xb1=stA["buf_use"][1],
                   psr=stA["ev1"] if False else None, psr2=None,
                   row=None, xh=None)
        st1["psr"] = stA["st1"]
        st1["psr2"] = stA["st2"]
        st1["row"] = stA["st2"]
        st1["xh"] = stA["xh_st"]
        st1["sq0"] = stA["sq_t"][0]
        st1["sq1"] = stA["sq_t"][1]
        stB = emit_stats(xfo, TOWN // 512, r1o_s, m1no_s, xhato_s, st1)
        stats_xb_guard = tmax(stA["buf_use"][0], stA["buf_use"][1],
                              stB["buf_use"][0], stB["buf_use"][1])
        stats_done = tmax(stA["st1"], stA["st2"], stA["xh_st"],
                          stB["st1"], stB["st2"], stB["xh_st"],
                          stB["rr_use"][0], stB["rr_use"][1])

    qkv_done = None
    qkv_wres_use = None
    if "qkv" in ph:
        wld32 = [w.ap().bitcast(F32) for w in wld]
        mz2 = DS(out=m1o[1:2, :], in_=onesrow, slot=3)
        mn_ld = DG(out=m1o[0:1, :], in_=m1n_s, w=tmax(stats_done), slot=3)
        mno_ld = DG(out=m1oo[0:1, :], in_=m1no_s, w=tmax(stats_done), slot=3)
        mz3 = DS(out=m1oo[1:2, :], in_=onesrow[0:1, 0:TOWN], slot=3)
        sapb_t = V("tensor_copy", sapb[:], sap[:], w=tmax(mod_done))

        xg_bufs = xb + xg
        xg_use = [stats_xb_guard, stats_xb_guard, None, None]
        prev_wres_use = tmax(mod_xb_guard)
        prev_st = {i: None for i in range(4)}
        if "stats" in ph:
            prev_st[0] = stB["sq_t"][0]
            prev_st[1] = stB["sq_t"][1]
        wl_use = [None] * 4
        mm_last = None
        for (pname, wsrc, scr, ntt, xsrc, mrow) in (
                ("q", wqT, q_scr, TOWN // 512, xhato_s, None),
                ("k", wkT, k_scr, T // 512, xhat_s, None),
                ("v", wvT, v_scr, T // 512, xhat_s, None)):
            mrow = m1oo if pname == "q" else m1o
            sc_toks = []
            wsrc_r = wsrc.rearrange("(c p) o -> p c o", p=128)
            for dc in range(NCK):
                q4 = dc % 4
                land_r = wld[q4 // 2].ap()[:, (q4 % 2) * 2048:(q4 % 2 + 1) * 2048]
                land = wld32[q4 // 2][:, (q4 % 2) * 2048:(q4 % 2 + 1) * 2048]
                ld = DS(out=land_r, in_=wsrc_r[:, dc, :].bitcast(F32R),
                        w=tmax(wl_use[q4]), slot=q4)
                sc = V("tensor_scalar", wres[dc][:], land,
                       opca[:, dc:dc + 1], None, Alu.mult,
                       w=tmax(ld, mod_done, qkv_wres_use if pname == "q" else None,
                              prev_wres_use))
                wl_use[q4] = sc
                sc_toks.append(sc)
            ev_w = None
            ev_q = None
            for ot in range(4):
                mmw = None
                mmq = None
                for dc in range(NCK):
                    wtile = wres[dc].ap()[:, ot * 512:(ot + 1) * 512]
                    mmw = TE("matmul", psR[:], onesb[:], wtile,
                             start=(dc == 0), stop=(dc == NCK - 1),
                             w=tmax(sc_toks[dc], mz, mmw, mm_last,
                                    ev_w if dc == 0 else None))
                    mmq = TE("matmul", psR2[:], sapb[:, dc:dc + 1], wtile,
                             start=(dc == 0), stop=(dc == NCK - 1),
                             w=tmax(sapb_t, mmq, ev_q if dc == 0 else None))
                    mm_last = mmq
                ev_w = V("tensor_copy", wbar2[0:1, ot * 512:(ot + 1) * 512],
                         psR[:], w=[mmw])
                ev_q0 = V("tensor_copy", qbrow[0:1, ot * 512:(ot + 1) * 512],
                          psR2[:], w=[mmq])
                ev_q = DS(out=wbar2[1:2, ot * 512:(ot + 1) * 512],
                          in_=qbrow[0:1, ot * 512:(ot + 1) * 512],
                          w=[ev_q0], slot=3)
            fold_ready = tmax(ev_w, ev_q, mz2, mn_ld, mno_ld, mz3)
            xsrc_r = xsrc.rearrange("(c p) t -> p c t", p=128)
            ngrp = (ntt + 3) // 4
            for tg in range(ngrp):
                tts = [tg * 4 + j for j in range(4) if tg * 4 + j < ntt]
                ld_t = []
                for ji, tt in enumerate(tts):
                    ld = DG(out=xg_bufs[ji].ap()
                            .rearrange("p (c t) -> p c t", t=512),
                            in_=xsrc_r[:, :, tt * 512:(tt + 1) * 512],
                            w=tmax(xg_use[ji]), slot=ji)
                    ld_t.append(ld)
                for oc in range(NCK):
                    mm = [None] * len(tts)
                    for dc in range(NCK):
                        lw = wres[dc].ap()[:, oc * 128:(oc + 1) * 128]
                        for ji, tt in enumerate(tts):
                            rhs = xg_bufs[ji].ap()[:, dc * 512:(dc + 1) * 512]
                            mm[ji] = TE("matmul", psA[ji][:], lw, rhs,
                                        start=(dc == 0), stop=False,
                                        w=tmax(ld_t[ji], sc_toks[dc], mm[ji],
                                               prev_st[ji] if dc == 0 else None,
                                               fold_ready if dc == 0 else None))
                    for ji, tt in enumerate(tts):
                        mm[ji] = TE("matmul", psA[ji][:],
                                    wbar2[0:2, oc * 128:(oc + 1) * 128],
                                    mrow[0:2, tt * 512:(tt + 1) * 512],
                                    start=False, stop=True, w=tmax(mm[ji]))
                        ev = Cs("copy", ost[ji][:], psA[ji][:], w=[mm[ji]])
                        st = DS(out=scr[oc * 128:(oc + 1) * 128,
                                        tt * 512:(tt + 1) * 512],
                                in_=ost[ji][:], w=[ev], slot=ji)
                        prev_st[ji] = tmax(ev, st)
                for ji in range(len(tts)):
                    xg_use[ji] = tmax(*[t for t in
                                        ([mm[ji]] if mm[ji] else [])])
                    xg_use[ji] = tmax(mm[ji])
            prev_wres_use = tmax(*[xg_use[i] for i in range(4)])
        qkv_wres_use = prev_wres_use
        qkv_done = tmax(prev_wres_use, *[prev_st[i] for i in range(4)
                                         if prev_st[i]])

    attn_done = None
    attn_wres_use = None
    if "attn" in ph:
        afm = wres
        kh_sb = xb[0].ap()[:, 0:T]
        vh_sb = xb[1].ap()[:, 0:T]
        qh_sb = xg[0].ap()[:, 0:TOWN]
        pex = [xg[0].ap()[:, TOWN:TOWN + 512],
               xg[0].ap()[:, TOWN + 512:TOWN + 1024]]
        ptb = xg[0].ap()[:, TOWN + 1024:TOWN + 1536]
        aot = xg[0].ap()[:, TOWN + 1536:TOWN + 1664]
        vt_sb = xg[1].ap()[:, 0:32 * 129]
        kh_sb = xb[0].ap()[:, 0:T] if False else kh_sb
        vt_ones = V("memset",
                    vt_sb.rearrange("p (k c) -> p k c", c=129)[:, :, 128:129],
                    1.0, w=tmax(qkv_wres_use, qkv_done))
        pex_use = {0: None, 1: None}
        psb_use = {0: None, 1: None}
        pt_use = {0: None, 1: None, 2: None, 3: None}
        psA_use = {0: None, 1: None}
        psO_ev = None
        s_last = None
        av_last = None
        tr_last = None
        aev_last = None
        kv_rd_last = None
        for h in range(H):
            g0 = qkv_done if h == 0 else None
            qld = DG(out=qh_sb, in_=q_scr[h * 128:(h + 1) * 128, :],
                     w=tmax(g0, kv_rd_last), slot=0)
            kld = DG(out=kh_sb, in_=k_scr[h * 128:(h + 1) * 128, :],
                     w=tmax(g0, kv_rd_last), slot=1)
            vld = DG(out=vh_sb, in_=v_scr[h * 128:(h + 1) * 128, :],
                     w=tmax(g0, kv_rd_last), slot=2)
            vt_toks = []
            for kc in range(T // 128):
                bi = kc % 2
                tr = TE("transpose", psB[bi][:, 0:128],
                        vh_sb[:, kc * 128:(kc + 1) * 128], idb[:],
                        w=tmax(vld, tb1, psb_use[bi]))
                ev = V("tensor_copy",
                       vt_sb[:, kc * 129:kc * 129 + 128],
                       psB[bi][:, 0:128],
                       w=tmax(tr, vt_ones, av_last))
                psb_use[bi] = ev
                vt_toks.append(ev)
            vt_ready = tmax(*vt_toks)
            sq_reads = []
            for n in range(NQB):
                tiles = attn_tiles(n, h)
                kend = (2 * n + 2) * 128
                nch_total = sum((min(j0 + w, T) - max(j0, 0)) // 128
                                for (j0, w, m) in tiles)
                av_mm = None
                nch = 0
                for (j0, w, m) in tiles:
                    js = max(j0, 0)
                    wv = j0 + w - js
                    si = nch % 2
                    smm = TE("matmul", psA[si][:, 0:wv],
                             qh_sb[:, n * 128:(n + 1) * 128],
                             kh_sb[:, js:js + wv],
                             start=True, stop=True,
                             w=tmax(qld, kld, psA_use[si]))
                    s_last = smm
                    sq_reads.append(smm)
                    pi = nch % 2
                    bcol = BIAS_COLS[(h, m)]
                    ex = Cs("activation", pex[pi][:, 0:wv], psA[si][:, 0:wv],
                            Act.Exp, bias=bt_sb[:, bcol:bcol + 1], scale=ISQ,
                            w=tmax(smm, tb2, pex_use[pi]))
                    psA_use[si] = ex
                    mtoks = []
                    for cc in range(wv // 128):
                        gchunk = (js + cc * 128) // 128
                        mc = None
                        if gchunk == kend // 128 - 1:
                            mc = 128
                        elif gchunk == kend // 128 - 2:
                            mc = 0
                        if mc is not None:
                            mtoks.append(V("tensor_tensor",
                                           pex[pi][:, cc * 128:(cc + 1) * 128],
                                           pex[pi][:, cc * 128:(cc + 1) * 128],
                                           mk_sb[:, mc:mc + 128], Alu.mult,
                                           w=tmax(ex, tb4)))
                    exm = tmax(ex, *mtoks)
                    tile_tr = []
                    for cc in range(wv // 128):
                        bi = (nch + cc) % 2
                        tr = TE("transpose", psB[bi][:, 128:256],
                                pex[pi][:, cc * 128:(cc + 1) * 128],
                                idb[:], w=tmax(exm, psb_use[bi]))
                        tr_last = tr
                        tile_tr.append(tr)
                        pslot = (nch + cc) % 4
                        fcol = F_COLS[(h, (js + cc * 128 - j0) // 128)]
                        pev = V("tensor_scalar",
                                ptb[:, pslot * 128:(pslot + 1) * 128],
                                psB[bi][:, 128:256],
                                ft_sb[:, fcol:fcol + 1], None, Alu.mult,
                                w=tmax(tr, tb3, pt_use[pslot]))
                        psb_use[bi] = pev
                        kcg = (js + cc * 128) // 128
                        idx = nch + cc
                        av_mm = TE("matmul", psA[2][:, 0:129],
                                   ptb[:, pslot * 128:(pslot + 1) * 128],
                                   vt_sb[:, kcg * 129:(kcg + 1) * 129],
                                   start=(idx == 0), stop=(idx == nch_total - 1),
                                   w=tmax(pev, vt_ready,
                                          psO_ev if idx == 0 else None))
                        pt_use[pslot] = av_mm
                        av_last = av_mm
                    nch += wv // 128
                    pex_use[pi] = tmax(*tile_tr)
                rc = V("reciprocal", linv[:], psA[2][:, 128:129],
                       w=tmax(av_mm))
                ao = V("tensor_scalar", aot, psA[2][:, 0:128], linv[:],
                       None, Alu.mult, w=[rc])
                psO_ev = ao
                trA = TE("transpose", psB[0][:, 256:384], aot, idb[:],
                         w=tmax(ao, psb_use[0]))
                tr_last = trA
                aev = V("tensor_copy",
                        afm[h].ap()[:, n * 128:(n + 1) * 128],
                        psB[0][:, 256:384],
                        w=tmax(trA, qkv_wres_use if (h == 0 and n == 0) else None))
                psb_use[0] = aev
                aev_last = aev
            kv_rd_last = tmax(s_last, av_last, tr_last)
        attn_done = tmax(aev_last, av_last, psO_ev)
        attn_wres_use = aev_last

    oproj_done = None
    oproj_wres_use = None
    if "oproj" in ph:
        ga_rep = xg[1].ap().bitcast(F32)[:, 2048:4096]
        xtmt = xg[1].ap().bitcast(F32)[:, 0:1024]
        x2t = [xg[0].ap().bitcast(F32)[:, i * 512:(i + 1) * 512]
               for i in range(2)]
        x2ft = [xg[0].ap().bitcast(F32)[:, 1024 + i * 128:1024 + (i + 1) * 128]
                for i in range(2)]
        wo_g = [xb[i] for i in range(2)]
        gal = DS(out=ga_rep, in_=garow_s.broadcast_to([128, D]),
                 w=tmax(attn_done, kv_rd_last, mod_done), slot=3)
        woT_r = woT.rearrange("(c p) o -> p c o", p=128)
        x2_r = x2_s.rearrange("(n p) d -> p n d", p=128)
        x2f_r = x2fm_s.rearrange("(c p) t -> p c t", p=128)
        prev_ev = {0: None, 1: None}
        prev_ftev = {0: None, 1: None}
        prev_ftst = {0: None, 1: None}
        prev_pst = {0: None, 1: None}
        wo_use = [tmax(attn_done, kv_rd_last)] * 2
        xld_use = tmax(attn_done, gal)
        ps_tr_use = {0: None, 1: None}
        mm = {0: None, 1: None}
        for mtg in range(2):
            wld_toks = []
            for half in range(2):
                ld = DG(out=wo_g[half].ap()
                        .rearrange("p (c o) -> p c o", o=1024),
                        in_=woT_r[:, half * 8:(half + 1) * 8,
                                  mtg * 1024:(mtg + 1) * 1024],
                        w=tmax(wo_use[half]), slot=half)
                wld_toks.append(ld)
            for n in range(NQB):
                xld = DG(out=xtmt,
                         in_=xtm[n * 128:(n + 1) * 128,
                                 mtg * 1024:(mtg + 1) * 1024],
                         w=tmax(xld_use), slot=2)
                mm = {0: None, 1: None}
                for oc in range(NCK):
                    lw = afm[oc].ap()[:, n * 128:(n + 1) * 128]
                    for j in range(2):
                        rhs = wo_g[oc // 8].ap()[
                            :, (oc % 8) * 1024 + j * 512:
                            (oc % 8) * 1024 + (j + 1) * 512]
                        mm[j] = TE("matmul", psA[j][:], lw, rhs,
                                   start=(oc == 0), stop=(oc == NCK - 1),
                                   w=tmax(wld_toks[oc // 8], mm[j],
                                          prev_ev[j] if oc == 0 else None,
                                          attn_done if oc == 0 and n == 0
                                          and mtg == 0 else None))
                for j in range(2):
                    mt = mtg * 2 + j
                    e1 = V("tensor_tensor", x2t[j], psA[j][:],
                           ga_rep[:, mt * 512:(mt + 1) * 512], Alu.mult,
                           w=tmax(mm[j], gal, prev_ftev[j], prev_pst[j]))
                    e2 = V("tensor_tensor", x2t[j], x2t[j],
                           xtmt[:, j * 512:(j + 1) * 512], Alu.add,
                           w=tmax(e1, xld))
                    prev_ev[j] = e2
                    st = DS(out=x2_r[:, n, mt * 512:(mt + 1) * 512],
                            in_=x2t[j], w=[e2], slot=j)
                    prev_pst[j] = st
                    for q in range(4):
                        qq = q % 2
                        tr = TE("transpose", psA[2 + qq][:, 0:128],
                                x2t[j][:, q * 128:(q + 1) * 128],
                                idf[:], w=tmax(e2, tb6, ps_tr_use[qq]))
                        ev = V("tensor_copy", x2ft[qq],
                               psA[2 + qq][:, 0:128],
                               w=tmax(tr, prev_ftst[qq]))
                        fst = DS(out=x2f_r[:, mt * 4 + q,
                                           n * 128:(n + 1) * 128],
                                 in_=x2ft[qq], w=[ev], slot=2 + qq)
                        ps_tr_use[qq] = ev
                        prev_ftev[j] = tmax(prev_ftev[j], tr)
                        prev_ftst[qq] = fst
                xld_use = tmax(*[prev_ev[j] for j in range(2)])
            wo_use = [tmax(mm[0], mm[1])] * 2
        oproj_done = tmax(prev_pst[0], prev_pst[1], prev_ftst[0],
                          prev_ftst[1], prev_ev[0], prev_ev[1])
        oproj_wres_use = tmax(mm[0], mm[1])

    QA_TH = {3: (16.0, 32.0, 15.4375), 1: (4.0, 8.0, None)}

    def emit_quant(dst, src, F, mfrac, bufs, w0, vname="vector"):
        nb = F // 32
        A, Bb, Cc, Kk = bufs["A"], bufs["B"], bufs["C"], bufs["K"]
        qb = qblk.ap()
        amax = qb[:, 0:nb]
        gg = qb[:, 64:64 + nb]
        cc = qb[:, 128:128 + nb]
        mm_ = qb[:, 192:192 + nb]
        fin = qb[:, 256:256 + nb]
        vmul = qb[:, 320:320 + nb]
        sd = qb[:, 384:384 + nb]
        src_v = src.rearrange("p (nb b) -> p nb b", b=32)
        t0 = V("tensor_reduce",
               amax.rearrange("p (nb one) -> p nb one", one=1), src_v,
               AX, Alu.max, apply_absolute_value=True, w=tmax(w0, bufs.get("qg")))
        t1 = V("tensor_scalar", gg, amax, 1e-30, None, Alu.max, w=[t0])
        t2 = V("tensor_scalar", cc, gg, 1.0, None, Alu.is_ge, w=[t1])
        t2b = V("tensor_scalar", mm_, cc, -1.0, 1.0, Alu.mult, Alu.add,
                w=[t2])
        t2c = V("tensor_scalar", mm_, mm_, 2.0 ** 100, None, Alu.mult,
                w=[t2b])
        t3 = V("tensor_tensor", mm_, mm_, cc, Alu.add, w=[t2c])
        t4 = V("tensor_tensor", gg, gg, mm_, Alu.mult, w=[t3])
        t5 = V("tensor_copy", fin, mm_, w=tmax(t3, t4))
        tk = tmax(t4, t5)
        for k in (64, 32, 16, 8, 4, 2, 1):
            u1 = V("tensor_scalar", cc, gg, float(2 ** k), None, Alu.is_ge,
                   w=tmax(tk))
            u2a = V("tensor_scalar", mm_, cc, 2.0 ** (-k), None, Alu.mult,
                    w=[u1])
            u2b = V("tensor_scalar", cc, cc, -1.0, 1.0, Alu.mult, Alu.add,
                    w=[u2a])
            u2 = V("tensor_tensor", mm_, mm_, cc, Alu.add, w=[u2b])
            u3 = V("tensor_tensor", gg, gg, mm_, Alu.mult, w=[u2])
            u4 = V("tensor_tensor", fin, fin, mm_, Alu.mult, w=[u2, u3])
            tk = tmax(u3, u4)
        v1 = V("tensor_scalar", vmul, fin, 2.0 ** (2 + mfrac), None, Alu.mult,
               w=tmax(tk))
        v2 = V("reciprocal", gg, fin, w=tmax(tk))
        v3 = V("tensor_scalar", sd, gg, 2.0 ** (-2 - mfrac), None, Alu.mult,
               w=[v2])
        vb = vmul.rearrange("p (nb one) -> p nb one", one=1).broadcast_to(
            [128, nb, 32])
        sdb = sd.rearrange("p (nb one) -> p nb one", one=1).broadcast_to(
            [128, nb, 32])
        th1, th2, clip = QA_TH[mfrac]
        q0 = V("tensor_tensor", A.rearrange("p (nb b) -> p nb b", b=32),
               src_v, vb, Alu.mult, w=tmax(v1, w0, bufs.get("qA")))
        q1a = V("tensor_scalar", Bb, A, -1.0, None, Alu.mult,
                w=tmax(q0, bufs.get("qB")))
        q1 = V("tensor_tensor", Bb, Bb, A, Alu.max, w=[q1a])
        q2 = V("tensor_scalar", Cc, Bb, th1, 1.0, Alu.is_ge, Alu.add,
               w=tmax(q1, bufs.get("qC")))
        q3 = V("tensor_scalar", A, Bb, th2, 1.0, Alu.is_ge, Alu.add, w=[q2])
        q4 = V("tensor_tensor", Cc, Cc, A, Alu.mult, w=[q3])
        q5 = V("reciprocal", A, Cc, w=[q4])
        q6 = V("tensor_tensor", Bb, Bb, A, Alu.mult, w=[q5])
        tq = q6
        if clip is not None:
            tq = V("tensor_scalar", Bb, Bb, clip, None, Alu.min, w=[q6])
        if mfrac == 3:
            p0 = V("tensor_scalar", A, Bb, 7.5, None, Alu.is_ge, w=[tq])
            p1 = V("tensor_scalar", Kk, A, 8.0, None, Alu.mult,
                   w=tmax(p0, bufs.get("qK")))
            p2 = V("scalar_tensor_tensor", Bb, A, -8.0, Bb, Alu.mult,
                   Alu.add, w=[p1])
            tq = p2
            for pk in (4.0, 2.0, 1.0):
                c1 = V("tensor_scalar", A, Bb, pk - 0.5, None, Alu.is_ge,
                       w=[tq])
                c2 = V("scalar_tensor_tensor", Kk, A, pk, Kk, Alu.mult,
                       Alu.add, w=[c1])
                c3 = V("scalar_tensor_tensor", Bb, A, -pk, Bb, Alu.mult,
                       Alu.add, w=tmax(c1, c2))
                tq = c3
            kt = tmax(tq, c2)
        else:
            s0 = V("tensor_scalar", Kk, Bb, 0.5, None, Alu.is_gt,
                   w=tmax(tq, bufs.get("qK")))
            s1 = V("tensor_scalar", A, Bb, 1.5, None, Alu.is_ge, w=[tq])
            s2 = V("tensor_tensor", Kk, Kk, A, Alu.add, w=tmax(s0, s1))
            s3 = V("tensor_scalar", A, Bb, 2.5, None, Alu.is_gt, w=[s2])
            s4 = V("tensor_tensor", Kk, Kk, A, Alu.add, w=[s3])
            s5 = V("tensor_scalar", A, Bb, 3.5, None, Alu.is_ge, w=[s4])
            s6 = V("tensor_tensor", Kk, Kk, A, Alu.add, w=[s5])
            kt = s6
        w1 = V("tensor_tensor", Kk, Kk, Cc, Alu.mult, w=tmax(kt))
        if mfrac == 1:
            w1 = V("tensor_scalar", Kk, Kk, 12.0, None, Alu.min, w=[w1])
        g0 = V("tensor_scalar", A, src, 0.0, 2.0, Alu.is_ge, Alu.mult,
               w=tmax(w1))
        g1 = V("tensor_scalar", A, A, -1.0, None, Alu.add, w=[g0])
        g2 = V("tensor_tensor", Kk, Kk, A, Alu.mult, w=[g1, w1])
        g3 = V("tensor_tensor", dst.rearrange("p (nb b) -> p nb b", b=32),
               Kk.rearrange("p (nb b) -> p nb b", b=32), sdb, Alu.mult,
               w=tmax(g2, v3, bufs.get("qdst")))
        return dict(tok=g3, qg=g3, qA=g3, qB=g3, qC=g3, qK=g3, qdst=g3,
                    srcfree=tmax(q0, g0))

    h2_done = None
    h2_wres_use = None
    if "h2" in ph:
        xb0f = xb[0].ap().bitcast(F32)
        xb1f = xb[1].ap().bitcast(F32)
        xg0f = xg[0].ap().bitcast(F32)
        xg1f = xg[1].ap().bitcast(F32)
        QA = xb0f[:, 0:1024]
        QB = xb0f[:, 1024:2048]
        QC = xb0f[:, 2048:3072]
        QK = xb0f[:, 3072:4096]
        x2src = xg1f[:, 0:2048]
        h2v = xg1f[:, 2048:4096]
        cbrep = xb1f[:, 0:2048]
        sbrep = xb1f[:, 2048:4096]
        ostq = xg[0].ap()[:, 0:2048]
        g_all = tmax(oproj_done, oproj_wres_use)
        cb_ld = DS(out=cbrep, in_=cbrow_s.broadcast_to([128, D]),
                   w=tmax(g_all, qkv_done), slot=2)
        sb_ld = DS(out=sbrep, in_=sbrow_s.broadcast_to([128, D]),
                   w=tmax(g_all), slot=2)
        qbufs = dict(A=QA, B=QB, C=QC, K=QK, qg=g_all, qA=g_all, qB=g_all,
                     qC=g_all, qK=g_all, qdst=None)
        x2_r2 = x2_s.rearrange("(n p) d -> p n d", p=128)
        prev_tr = {0: None, 1: None}
        prev_wr = None
        xld_g = tmax(g_all, cb_ld, sb_ld)
        ost_g = None
        stt_g = None
        for n in range(NQB):
            ld = DS(out=x2src, in_=x2_r2[:, n, :], w=tmax(xld_g), slot=3)
            r0 = V("tensor_reduce",
                   stt2.ap()[:, 0:1].rearrange("p (a one) -> p a one", one=1),
                   x2src.rearrange("p (a d) -> p a d", a=1), AX, Alu.add,
                   w=tmax(ld, stt_g))
            sqscr = xb0f[:, 0:2048]
            sq0 = Cs("activation", sqscr, x2src, Act.Square,
                     w=tmax(ld, qbufs["qA"], qbufs["qB"]))
            r1 = V("tensor_reduce",
                   stt2.ap()[:, 1:2].rearrange("p (a one) -> p a one", one=1),
                   sqscr.rearrange("p (a d) -> p a d", a=1), AX, Alu.add,
                   w=tmax(sq0, r0))
            qbufs["qA"] = tmax(qbufs["qA"], r1)
            qbufs["qB"] = tmax(qbufs["qB"], r1)
            m0 = V("tensor_scalar", stt2.ap()[:, 2:3], stt2.ap()[:, 0:1], 1.0 / D, None,
                   Alu.mult, w=[r1])
            m1 = V("tensor_scalar", stt2.ap()[:, 3:4], stt2.ap()[:, 1:2], 1.0 / D, None,
                   Alu.mult, w=[m0])
            m2 = Cs("activation", stt2.ap()[:, 4:5], stt2.ap()[:, 2:3], Act.Square,
                    w=[m1])
            m3 = V("tensor_tensor", stt2.ap()[:, 3:4], stt2.ap()[:, 3:4], stt2.ap()[:, 4:5],
                   Alu.subtract, w=[m2])
            m4 = V("tensor_scalar", stt2.ap()[:, 3:4], stt2.ap()[:, 3:4], EPS, None,
                   Alu.add, w=[m3])
            m5 = V("reciprocal", stt2.ap()[:, 4:5], stt2.ap()[:, 3:4], w=[m4])
            m6 = Cs("activation", stt2.ap()[:, 5:6], stt2.ap()[:, 4:5], Act.Sqrt,
                    w=[m5])
            h0 = V("tensor_scalar", h2v, x2src, stt2.ap()[:, 2:3], stt2.ap()[:, 5:6],
                   Alu.subtract, Alu.mult, w=tmax(m6, qbufs.get("h2v_g")))
            h1 = V("tensor_tensor", h2v, h2v, cbrep, Alu.mult,
                   w=tmax(h0, cb_ld))
            h2t = V("tensor_tensor", h2v, h2v, sbrep, Alu.add,
                    w=tmax(h1, sb_ld))
            stt_g = h0
            res = emit_quant(ostq[:, 0:1024], h2v[:, 0:1024], 1024, 3,
                             qbufs, tmax(h2t, ost_g))
            qbufs.update(qg=res["qg"], qA=res["qA"], qB=res["qB"],
                         qC=res["qC"], qK=res["qK"])
            res = emit_quant(ostq[:, 1024:2048], h2v[:, 1024:2048], 1024, 3,
                             qbufs, tmax(h2t, res["tok"]))
            qbufs.update(qg=res["qg"], qA=res["qA"], qB=res["qB"],
                         qC=res["qC"], qK=res["qK"])
            qbufs["h2v_g"] = res["srcfree"]
            xld_g = res["srcfree"]
            trs = []
            for dc in range(NCK):
                bi = dc % 2
                tr = TE("transpose", psB[bi][:, 384:512],
                        ostq[:, dc * 128:(dc + 1) * 128], idb[:],
                        w=tmax(res["tok"], prev_tr[bi], tb1))
                ev = V("tensor_copy", wres[dc].ap()[:, n * 128:(n + 1) * 128],
                       psB[bi][:, 384:512],
                       w=tmax(tr, oproj_wres_use if n == 0 else None,
                              prev_wr if dc == 0 else None))
                prev_tr[bi] = ev
                trs.append(ev)
            prev_wr = tmax(*trs)
            ost_g = tmax(*trs)
        h2_done = prev_wr
        h2_wres_use = prev_wr

    wq_done = None
    if "wq" in ph:
        xb0f = xb[0].ap().bitcast(F32)
        xb1f = xb[1].ap().bitcast(F32)
        xg1f = xg[1].ap().bitcast(F32)
        QA = xb0f[:, 0:2048]
        QB = xb0f[:, 2048:4096]
        QC = xb1f[:, 0:2048]
        QK = xb1f[:, 2048:4096]
        wsrc_b = xg1f[:, 0:2048]
        wv_b = xg1f[:, 2048:4096]
        ostq = xg[0].ap()[:, 0:2048]
        stage = xg[0].ap()[:, 2048:4096]
        g0 = tmax(h2_done, h2_wres_use)
        qbufs = dict(A=QA, B=QB, C=QC, K=QK, qg=g0, qA=g0, qB=g0, qC=g0,
                     qK=g0, qdst=g0)
        qwg_r = qwg_s.rearrange("(c p) i -> p c i", p=128)
        qwu_r = qwu_s.rearrange("(c p) i -> p c i", p=128)
        qwd_r = qwd_s.rearrange("(c p) o -> p c o", p=128)
        ld_g = g0
        prev_tr = {0: None, 1: None}
        prev_stg = None
        stg_st = None
        for wi, wname in enumerate(("wg", "wu", "wd")):
            wsrc = (wg, wu, wd)[wi]
            for ti in range(64):
                if wname == "wd":
                    ri, cs = ti % 16, ti // 16
                    src_ap = wd[ri * 128:(ri + 1) * 128,
                                cs * 2048:(cs + 1) * 2048]
                else:
                    src_ap = wsrc[ti * 128:(ti + 1) * 128, :]
                ld = DS(out=wsrc_b, in_=src_ap, w=tmax(ld_g), slot=3)
                res = emit_quant(ostq, wsrc_b, D, 1, qbufs, tmax(ld))
                qbufs.update(qg=res["qg"], qA=res["qA"], qB=res["qB"],
                             qC=res["qC"], qK=res["qK"], qdst=res["qdst"])
                ld_g = res["srcfree"]
                trs = []
                for jc in range(NCK):
                    bi = jc % 2
                    tr = TE("transpose", psB[bi][:, 384:512],
                            ostq[:, jc * 128:(jc + 1) * 128], idb[:],
                            w=tmax(res["tok"], prev_tr[bi], tb1))
                    ev = V("tensor_copy", stage[:, jc * 128:(jc + 1) * 128],
                           psB[bi][:, 384:512],
                           w=tmax(tr, stg_st if jc == 0 else None))
                    prev_tr[bi] = ev
                    trs.append(ev)
                qbufs["qdst"] = tmax(*[t for t in trs])
                if wname == "wg":
                    out_ap = qwg_r[:, :, ti * 128:(ti + 1) * 128]
                elif wname == "wu":
                    out_ap = qwu_r[:, :, ti * 128:(ti + 1) * 128]
                else:
                    out_ap = qwd_r[:, cs * 16:(cs + 1) * 16,
                                   ri * 128:(ri + 1) * 128]
                stg_st = DS(out=out_ap,
                            in_=stage.rearrange("p (c t) -> p c t", t=128),
                            w=tmax(*trs), slot=ti % 2)
        wq_done = tmax(stg_st, qbufs["qdst"], ld_g)

    gate_done = None
    gate_wres_use = None
    if "gate" in ph:
        qwgrp = xb + xg
        sgr = sg_s.rearrange("(n p) i -> p n i", p=128)
        g0 = tmax(wq_done)
        grp_use = [g0] * 4
        prev_ev = {j: None for j in range(4)}
        prev_pst = {j: None for j in range(4)}
        mm_last = None
        for Ig in range(4):
            lds = []
            for bi in range(4):
                ld = DG(out=qwgrp[bi].ap().rearrange("p (c i) -> p c i", i=2048),
                        in_=qwg_s.rearrange("(c p) i -> p c i", p=128)
                        [:, bi * 4:(bi + 1) * 4, Ig * 2048:(Ig + 1) * 2048],
                        w=tmax(grp_use[bi]), slot=bi)
                lds.append(ld)
            for tt in range(NQB):
                mm = {j: None for j in range(4)}
                for dc in range(NCK):
                    lw = wres[dc].ap()[:, tt * 128:(tt + 1) * 128]
                    for j in range(4):
                        rhs = qwgrp[dc // 4].ap()[
                            :, (dc % 4) * 2048 + j * 512:
                            (dc % 4) * 2048 + (j + 1) * 512]
                        mm[j] = TE("matmul", psA[j][:], lw, rhs,
                                   start=(dc == 0), stop=(dc == NCK - 1),
                                   w=tmax(lds[dc // 4], mm[j],
                                          prev_ev[j] if dc == 0 else None,
                                          h2_wres_use if dc == 0 and Ig == 0
                                          and tt == 0 else None))
                for j in range(4):
                    ev = Cs("activation", ost[j][:], psA[j][:], Act.Silu,
                            w=tmax(mm[j], prev_pst[j]))
                    prev_ev[j] = ev
                    st = DS(out=sgr[:, tt, (Ig * 4 + j) * 512:
                                    (Ig * 4 + j + 1) * 512],
                            in_=ost[j][:], w=[ev], slot=j)
                    prev_pst[j] = st
                mm_last = tmax(*[mm[j] for j in range(4)])
            grp_use = [mm_last] * 4
        gate_done = tmax(mm_last, *[prev_pst[j] for j in range(4)])
        gate_wres_use = mm_last

    up_done = None
    up_wres_use = None
    if "up" in ph:
        xg0f = xg[0].ap().bitcast(F32)
        xg1f = xg[1].ap().bitcast(F32)
        UA = xg1f[:, 0:512]
        UB = xg1f[:, 512:1024]
        UC = xg1f[:, 1024:1536]
        UK = xg1f[:, 1536:2048]
        uint = xg1f[:, 2048:2560]
        sgt = xg[1].ap()[:, 5120:5632]
        uq = xg[0].ap()[:, 0:512]
        ustage = xg[0].ap()[:, 512:1024]
        g0 = tmax(gate_done, gate_wres_use)
        qbufs = dict(A=UA, B=UB, C=UC, K=UK, qg=g0, qA=g0, qB=g0, qC=g0,
                     qK=g0, qdst=g0)
        qir = qint_s.rearrange("(c p) t -> p c t", p=128)
        sgr = sg_s.rearrange("(n p) i -> p n i", p=128)
        grp_use = [g0] * 2
        prev_ev = {j: None for j in range(2)}
        sg_use = None
        prev_tr = {0: None, 1: None}
        stg_st = None
        ld_g = g0
        mm_last = None
        for Ig in range(8):
            lds = []
            for bi in range(2):
                ld = DG(out=xb[bi].ap().rearrange("p (c i) -> p c i", i=1024),
                        in_=qwu_s.rearrange("(c p) i -> p c i", p=128)
                        [:, bi * 8:(bi + 1) * 8, Ig * 1024:(Ig + 1) * 1024],
                        w=tmax(grp_use[bi]), slot=bi)
                lds.append(ld)
            for tt in range(NQB):
                mm = {j: None for j in range(2)}
                for dc in range(NCK):
                    lw = wres[dc].ap()[:, tt * 128:(tt + 1) * 128]
                    for j in range(2):
                        bi = dc // 8
                        rhs = xb[bi].ap()[
                            :, (dc % 8) * 1024 + j * 512:
                            (dc % 8) * 1024 + (j + 1) * 512]
                        mm[j] = TE("matmul", psA[j][:], lw, rhs,
                                   start=(dc == 0), stop=(dc == NCK - 1),
                                   w=tmax(lds[bi], mm[j],
                                          prev_ev[j] if dc == 0 else None,
                                          gate_wres_use if dc == 0 and Ig == 0
                                          and tt == 0 else None))
                for j in range(2):
                    It = Ig * 2 + j
                    sld = DG(out=sgt, in_=sgr[:, tt, It * 512:(It + 1) * 512],
                             w=tmax(sg_use, gate_done if Ig == 0 else None),
                             slot=2)
                    iv = V("tensor_tensor", uint, psA[j][:], sgt, Alu.mult,
                           w=tmax(mm[j], sld, qbufs.get("uint_g")))
                    prev_ev[j] = iv
                    sg_use = iv
                    res = emit_quant(uq, uint, 512, 3, qbufs, tmax(iv))
                    qbufs.update(qg=res["qg"], qA=res["qA"], qB=res["qB"],
                                 qC=res["qC"], qK=res["qK"])
                    qbufs["uint_g"] = res["srcfree"]
                    trs = []
                    for jc in range(4):
                        bi2 = jc % 2
                        tr = TE("transpose", psB[bi2][:, 384:512],
                                uq[:, jc * 128:(jc + 1) * 128], idb[:],
                                w=tmax(res["tok"], prev_tr[bi2], tb1))
                        ev = V("tensor_copy",
                               ustage[:, jc * 128:(jc + 1) * 128],
                               psB[bi2][:, 384:512],
                               w=tmax(tr, stg_st if jc == 0 else None))
                        prev_tr[bi2] = ev
                        trs.append(ev)
                    qbufs["qdst"] = tmax(*trs)
                    stg_st = DS(out=qir[:, It * 4:(It + 1) * 4,
                                        tt * 128:(tt + 1) * 128],
                                in_=ustage.rearrange("p (c t) -> p c t", t=128),
                                w=tmax(*trs), slot=3)
                mm_last = tmax(*[mm[j] for j in range(2)])
            grp_use = [mm_last] * 2
        up_done = tmax(mm_last, stg_st, qbufs["qdst"])
        up_wres_use = mm_last

    down_done = None
    if "down" in ph:
        yst = [ost[i].ap().bitcast(F32)[:, 0:256] for i in range(2)]
        x2f = [ost[2 + i].ap().bitcast(F32)[:, 0:256] for i in range(2)]
        qcolb = [(xg[0], xb[0]), (xg[1], xb[1])]
        g0 = tmax(up_done, up_wres_use, h2_done)
        wres_use = [g0] * NCK
        qc_use = [g0, g0]
        y_r = y
        x2f_r2 = x2fm_s
        prev_yev = {0: None, 1: None}
        prev_yst = {0: None, 1: None}
        x2_use = {0: None, 1: None}
        mm_fin = None
        for oq in range(4):
            wlds = []
            for i in range(NCK):
                ld = DG(out=wres[i].ap().rearrange("p (c o) -> p c o", o=512),
                        in_=qwd_s.rearrange("(c p) o -> p c o", p=128)
                        [:, i * 4:(i + 1) * 4, oq * 512:(oq + 1) * 512],
                        w=tmax(wres_use[i]), slot=i % 4)
                wlds.append(ld)
            for tb in range(TOWN // 256):
                cb_ = tb % 2
                qa, qb_ = qcolb[cb_]
                qld1 = DG(out=qa.ap()[:, 0:8192]
                          .rearrange("p (c t) -> p c t", t=256),
                          in_=qint_s.rearrange("(c p) t -> p c t", p=128)
                          [:, 0:32, tb * 256:(tb + 1) * 256],
                          w=tmax(qc_use[cb_]), slot=cb_)
                qld2 = DG(out=qb_.ap()[:, 0:8192]
                          .rearrange("p (c t) -> p c t", t=256),
                          in_=qint_s.rearrange("(c p) t -> p c t", p=128)
                          [:, 32:64, tb * 256:(tb + 1) * 256],
                          w=tmax(qc_use[cb_]), slot=2 + cb_)
                for ochk in range(4):
                    jj = ochk % 2
                    oc16 = oq * 4 + ochk
                    mm = None
                    for ic in range(64):
                        lw = wres[ic // 4].ap()[
                            :, (ic % 4) * 512 + ochk * 128:
                            (ic % 4) * 512 + (ochk + 1) * 128]
                        src = (qa if ic < 32 else qb_).ap()[
                            :, (ic % 32) * 256:(ic % 32 + 1) * 256]
                        mm = TE("matmul", psA[jj][:, 0:256], lw, src,
                                start=(ic == 0), stop=(ic == 63),
                                w=tmax(wlds[ic // 4], qld1, qld2, mm,
                                       prev_yev[jj] if ic == 0 else None))
                    mm_fin = mm
                    x2ld = DG(out=x2f[jj],
                              in_=x2fm_s[oc16 * 128:(oc16 + 1) * 128,
                                         tb * 256:(tb + 1) * 256],
                              w=tmax(x2_use[jj], oproj_done if oq == 0 and
                                     tb == 0 else None), slot=2 + jj)
                    y1 = V("tensor_scalar", yst[jj], psA[jj][:, 0:256],
                           gbcl[:, oc16:oc16 + 1], None, Alu.mult,
                           w=tmax(mm, prev_yst[jj], mod_done))
                    y2 = V("tensor_tensor", yst[jj], yst[jj], x2f[jj],
                           Alu.add, w=tmax(y1, x2ld))
                    x2_use[jj] = y2
                    prev_yev[jj] = y2
                    yt = DS(out=y[oc16 * 128:(oc16 + 1) * 128,
                                  tb * 256:(tb + 1) * 256],
                            in_=yst[jj], w=[y2], slot=jj)
                    prev_yst[jj] = yt
                qc_use[cb_] = mm_fin
            wres_use = [mm_fin] * NCK
        down_done = tmax(prev_yst[0], prev_yst[1])

    for sname in Plan.SEMS:
        if pl.cnt[sname] > 0:
            pl.wait("sync", (sname, pl.cnt[sname]))

    sems = {}
    for sname in Plan.SEMS:
        sems[sname] = e(nc.semaphore(sname))
    block = e(nc.Block())

    @block.sync
    def _(se):
        pl.emit(se, "sync", sems)

    @block.gpsimd
    def _(ge):
        pl.emit(ge, "gpsimd", sems)

    @block.tensor
    def _(te):
        pl.emit(te, "tensor", sems)

    @block.vector
    def _(ve):
        pl.emit(ve, "vector", sems)

    @block.scalar
    def _(ce):
        pl.emit(ce, "scalar", sems)

    stack.close()
    return nc



def make_in_maps(x, wqT, wkT, wvT, woT, wg, wu, wd, wmod, c):
    identb = np.eye(128, dtype=ml_dtypes.bfloat16)
    in_maps = []
    for core in range(8):
        b, p = core // 2, core % 2
        rows = own_rows(p)
        bt, ft, msk = host_tables(p)
        in_maps.append(dict(
            xfm=np.ascontiguousarray(x[b].T),
            xfo=np.ascontiguousarray(x[b][rows].T),
            xtm=np.ascontiguousarray(x[b][rows]),
            wqT=wqT, wkT=wkT, wvT=wvT, woT=woT,
            wg=wg, wu=wu, wd=wd, wmod=wmod,
            ccol=np.ascontiguousarray(c[b].reshape(NCK, 128).T),
            btab=bt, ftab=ft, mtab=msk, identb=identb,
            onesrow=np.ones((1, T), dtype=ml_dtypes.bfloat16),
            identf=np.eye(128, dtype=np.float32),
        ))
    return in_maps


def kernel(L0_attn_x, L0_attn_Wq, L0_attn_Wk, L0_attn_Wv, L0_attn_Wo,
           L0_ffn_w_gate, L0_ffn_w_up, L0_ffn_w_down, L0_c, L0_wmod):
    x = np.ascontiguousarray(np.asarray(L0_attn_x))
    wqT = np.ascontiguousarray(np.asarray(L0_attn_Wq).T)
    wkT = np.ascontiguousarray(np.asarray(L0_attn_Wk).T)
    wvT = np.ascontiguousarray(np.asarray(L0_attn_Wv).T)
    woT = np.ascontiguousarray(np.asarray(L0_attn_Wo).T)
    wg = np.ascontiguousarray(np.asarray(L0_ffn_w_gate))
    wu = np.ascontiguousarray(np.asarray(L0_ffn_w_up))
    wd = np.ascontiguousarray(np.asarray(L0_ffn_w_down))
    wmod = np.ascontiguousarray(np.asarray(L0_wmod))
    c = np.ascontiguousarray(np.asarray(L0_c))

    nc = build()
    in_maps = make_in_maps(x, wqT, wkT, wvT, woT, wg, wu, wd, wmod, c)
    res = run_bass_kernel_spmd(nc, in_maps, core_ids=list(range(8)))
    out = np.empty((B, T, D), dtype=np.float32)
    for core in range(8):
        b, p = core // 2, core % 2
        out[b][own_rows(p)] = np.asarray(res.results[core]["y"]).T
    return out
